# version 01 (final)
# baseline (speedup 1.0000x reference)
import numpy as np
import ml_dtypes
from contextlib import ExitStack

import concourse.bass as bass
import concourse.mybir as mybir
from concourse.bass_utils import run_bass_kernel_spmd

F32 = mybir.dt.float32
BF16 = mybir.dt.bfloat16
AF = mybir.ActivationFunctionType
ALU = mybir.AluOpType
AX = mybir.AxisListType

HID = 2048
OUT_IN = 8512
H = 64
CH = 256
NKT = 16
TOK = 1024
NSL = 4
WIN = TOK + 3
XROWS = 3 + 3 * TOK + TOK
NE = 16
EPS = 1e-5
TILE_B, TILE_C, TILE_DT = 64, 65, 66
ENGINES = ("tensor", "vector", "scalar", "gpsimd", "sync")

CFG = {}


class Ctr:
    def __init__(self, name, step=1):
        self.name, self.n, self.step = name, 0, step


class Prog:
    def __init__(self):
        self.streams = {e: [] for e in ENGINES}
        self.ctrs = {}

    def ctr(self, name, step=1):
        if name not in self.ctrs:
            self.ctrs[name] = Ctr(name, step)
        return self.ctrs[name]

    def emit(self, eng, fn, wait=(), inc=None):
        w = []
        seen = {}
        for item in wait:
            if item is None:
                continue
            c, v = item
            assert v <= c.n, f"wait {c.name}>={v} but only {c.n} (deadlock)"
            if v > 0 and seen.get(c.name, -1) < v:
                seen[c.name] = v
        w = list(seen.items())
        n = None
        if inc is not None:
            inc.n += inc.step
            n = inc.n
        self.streams[eng].append((tuple(w), fn, inc.name if inc else None,
                                  inc.step if inc else 0))
        return n


def build_nc(cfg=None):
    cfg = dict(CFG, **(cfg or {}))
    n_experts = cfg.get("n_experts", NE)
    do_shared_f = cfg.get("do_shared", True)
    n_slices = cfg.get("n_slices", NSL)
    dumps = cfg.get("dumps", [])

    nc = bass.Bass("TRN2", target_bir_lowering=False,
                   detect_race_conditions=bool(cfg.get("rd")))

    dx = nc.dram_tensor("x_core", [XROWS, HID], F32, kind="ExternalInput").ap()
    dmask_ln = nc.dram_tensor("mask_ln", [128, NSL * 9], F32, kind="ExternalInput").ap()
    dmask_blk = nc.dram_tensor("mask_blk", [128, NSL * 8], F32, kind="ExternalInput").ap()
    dwin = nc.dram_tensor("w_in_dev", [67, 128, NKT * 128], F32, kind="ExternalInput").ap()
    dconvw = nc.dram_tensor("conv_w_fm", [128, 34 * 4], F32, kind="ExternalInput").ap()
    dconvb = nc.dram_tensor("conv_b_fm", [128, 34], F32, kind="ExternalInput").ap()
    ddtb = nc.dram_tensor("dt_bias", [64, 1], F32, kind="ExternalInput").ap()
    dalog = nc.dram_tensor("a_log", [64, 1], F32, kind="ExternalInput").ap()
    ddexp = nc.dram_tensor("d_exp", [128, 32], F32, kind="ExternalInput").ap()
    dgam = nc.dram_tensor("gamma_fm", [128, 32], F32, kind="ExternalInput").ap()
    dwout = nc.dram_tensor("w_out_dev", [32, 128, HID], F32, kind="ExternalInput").ap()
    dgate = nc.dram_tensor("gate_dev", [128, NKT * 16], F32, kind="ExternalInput").ap()
    dup = nc.dram_tensor("up_dev", [NE, 22, 128, NKT * 128], F32, kind="ExternalInput").ap()
    ddn = nc.dram_tensor("down_dev", [NE, 11, 128, HID], F32, kind="ExternalInput").ap()
    dsup = nc.dram_tensor("shup_dev", [44, 128, NKT * 128], F32, kind="ExternalInput").ap()
    dsdn = nc.dram_tensor("shdn_dev", [22, 128, HID], F32, kind="ExternalInput").ap()
    dtri = nc.dram_tensor("tri", [128, 128], F32, kind="ExternalInput").ap()
    didentb = nc.dram_tensor("identb", [128, 128], BF16, kind="ExternalInput").ap()
    didentf = nc.dram_tensor("identf", [128, 128], F32, kind="ExternalInput").ap()
    dconsts = nc.dram_tensor("consts", [128, 8], F32, kind="ExternalInput").ap()
    dones = nc.dram_tensor("ones_row", [1, 128], F32, kind="ExternalInput").ap()
    donesb = nc.dram_tensor("ones_col_b", [128, 1], BF16, kind="ExternalInput").ap()
    donesa = nc.dram_tensor("ones_all", [128, 128], F32, kind="ExternalInput").ap()

    dout = nc.dram_tensor("out", [TOK, HID], F32, kind="ExternalOutput").ap()
    dacs = nc.dram_tensor("acs_scr", [64, CH], F32).ap()
    dcw = nc.dram_tensor("cw_scr", [16, TOK], F32).ap()

    ddump = {}
    dump_shapes = {
        "xn": ([128, NKT * WIN], BF16), "xfm": ([128, 32 * TOK], BF16),
        "bfm": ([128, TOK], BF16), "cfm": ([128, TOK], BF16),
        "dt": ([64, TOK], F32), "r": ([128, 4096], F32),
        "y": ([128, 32 * TOK], BF16), "acc": ([128, 16 * TOK], F32),
        "cw": ([128, 128], F32), "a": ([128, 22 * TOK], BF16),
    }
    for tag in dumps:
        sh, dt = dump_shapes[tag]
        ddump[tag] = nc.dram_tensor("dump_" + tag, sh, dt, kind="ExternalOutput").ap()

    prog = Prog()
    T = prog.ctr("T")
    V = prog.ctr("V")
    A = prog.ctr("Aa")
    PL = prog.ctr("PL")
    LD = prog.ctr("LD", 16)
    LDS = [prog.ctr("LD0", 16), prog.ctr("LD1", 16)]
    LDC = prog.ctr("LDC", 16)
    SCA = prog.ctr("SCA", 16)
    SFS = [prog.ctr("SF0", 16), prog.ctr("SF1", 16)]
    SCW = prog.ctr("SCW", 16)
    SCR = prog.ctr("SCR", 16)
    SC = prog.ctr("SC", 16)
    ST = prog.ctr("ST", 16)
    E = prog.emit

    with ExitStack() as ES:
        def sbt(name, shape, dt):
            return ES.enter_context(nc.sbuf_tensor(name, shape, dt))

        ringF = [sbt(f"ringF{i}", [128, 2048], F32) for i in range(2)]
        ringB = [sbt(f"ringB{i}", [128, 2048], BF16) for i in range(2)]
        xn_fm = sbt("xn_fm", [128, NKT * WIN], BF16)
        big1 = sbt("big1", [128, 32 * TOK], BF16)
        big2 = sbt("big2", [128, 16384], F32)
        store_sb = sbt("store_sb", [128, 2048], F32)
        convw = sbt("convw", [128, 34 * 4], F32)
        convb = sbt("convb", [128, 34], F32)
        dtb = sbt("dtb", [64, 1], F32)
        alog = sbt("alog", [64, 1], F32)
        aneg = sbt("aneg", [64, 1], F32)
        dexp = sbt("dexp", [128, 32], F32)
        gam = sbt("gam", [128, 32], F32)
        tri = sbt("tri_s", [128, 128], F32)
        identb = sbt("identb_s", [128, 128], BF16)
        identf = sbt("identf_s", [128, 128], F32)
        consts = sbt("consts_s", [128, 8], F32)
        ones_f = sbt("ones_f", [1, 128], F32)
        ones_b = sbt("ones_cb", [128, 1], BF16)
        onesall = sbt("ones_all_s", [128, 128], F32)
        mask_ln = sbt("mask_ln_s", [128, NSL * 9], F32)
        mask_blk = sbt("mask_blk_s", [128, NSL * 8], F32)
        gate_b = sbt("gate_b", [128, NKT * 16], BF16)
        cw = sbt("cw_s", [128, 128], F32)
        ga = sbt("ga", [128, 16], F32)
        gb = sbt("gb", [128, 16], F32)
        gmx = sbt("gmx", [128, 4], F32)
        gmk = sbt("gmk", [128, 4], F32)
        sel = sbt("sel", [128, 16], F32)
        wrk = sbt("wrk", [128, 16], F32)
        lns = sbt("lns", [128, 6], F32)

        R = big2[:, 0:4096]
        Rb = big2[:, 4096:6144].bitcast(BF16)
        x_tm = [big2[:, 6144 + 2048 * i:6144 + 2048 * (i + 1)].bitcast(BF16)
                for i in range(2)]
        xn_tm = [big2[:, 6144 + 1024 * i:6144 + 1024 * (i + 1)].bitcast(BF16)
                 for i in range(2)]
        C0 = 10240
        dt_sl = big2[0:64, C0:C0 + 1024]
        dA_sl = big2[0:64, C0 + 1024:C0 + 2048]
        flat_t = [sbt(f"flat{i}", [1, 1024], F32) for i in range(2)]
        flat4 = [flat_t[0][0:1, :], flat_t[1][0:1, :]]
        Bfm = big2[:, C0 + 2048:C0 + 2560].bitcast(BF16)
        Cfm = big2[:, C0 + 2560:C0 + 3072].bitcast(BF16)
        CT = C0 + 3072
        cs = big2[:, CT:CT + 1028]
        co = big2[:, CT + 1028:CT + 2052]
        sqln = cs[:, 0:1024].bitcast(BF16)
        cur = [CT]

        def carve(ncols):
            c0_ = cur[0]
            cur[0] += ncols
            assert cur[0] <= 16384, f"big2 overflow {cur[0]}"
            return big2[:, c0_:c0_ + ncols]

        acs = carve(256)[0:64, :]
        acsT = carve(128)
        dtT = carve(128)
        wt_sb = carve(128)
        dec_sb = carve(64)
        erow = carve(64)[0:1, :]
        B_tm = carve(128).bitcast(BF16)
        GTzb = carve(192).bitcast(BF16)
        seg = carve(384)
        stg = carve(256)
        Mt = [carve(192).bitcast(BF16) for _ in range(2)]
        sd_h = [carve(128).bitcast(BF16) for _ in range(2)]
        yf_sb = carve(256)
        szp = [carve(128).bitcast(BF16) for _ in range(2)]
        vrow = stg[0:1, 0:256]
        yo_sb = seg[:, 0:256]
        sq_sb = szp
        assert cur[0] <= 16384

        acsT3 = acsT.rearrange("p (l h) -> p l h", l=2)
        dtT3 = dtT.rearrange("p (l h) -> p l h", l=2)
        xn3 = xn_fm[:, :].rearrange("p (k w) -> p k w", k=NKT)
        cwT = store_sb[0:16, 0:1024]
        cwrow = store_sb[32:33, 0:1024]
        cwb_sb = store_sb[:, 1024:2048]

        pb = [ES.enter_context(nc.psum_tensor(f"pb{i}", [128, 512], F32))
              for i in range(7)]
        pbt = ES.enter_context(nc.psum_tensor("pbt", [128, 1024], BF16))
        pbtf = pbt[:, :].bitcast(F32)

        def dbank(o):
            if o < 12:
                return pb[o // 2], 256 * (o % 2)
            elif o < 14:
                return pb[6], 256 * (o % 2)
            else:
                return pbtf, 256 * (o % 2)

        def acc_view(o, tq):
            return big2[:, o * TOK + 256 * tq:o * TOK + 256 * (tq + 1)]

        def dn_home(kt):
            if kt < 10:
                return big1[:, 11 * TOK + 2048 * kt:11 * TOK + 2048 * (kt + 1)]
            else:
                return cwb_sb.bitcast(BF16)

        st8 = {}
        st8["bar"] = []
        st8["bar_pending"] = set()

        def EB(eng, fn, wait=(), inc=None):
            w = list(wait)
            if eng in st8["bar_pending"]:
                w += st8["bar"]
                st8["bar_pending"].discard(eng)
            return E(eng, fn, wait=w, inc=inc)

        for dst, src in [
            (convw[:, :], dconvw), (convb[:, :], dconvb), (dtb[:, :], ddtb),
            (alog[:, :], dalog), (dexp[:, :], ddexp), (gam[:, :], dgam),
            (tri[:, :], dtri), (identb[:, :], didentb), (identf[:, :], didentf),
            (consts[:, :], dconsts), (ones_f[:, :], dones), (ones_b[:, :], donesb),
            (onesall[:, :], donesa),
            (mask_ln[:, :], dmask_ln), (mask_blk[:, :], dmask_blk),
        ]:
            E("sync", lambda q, d=dst, s=src: q.dma_start(d, s), inc=LDC)
        CONST_LD = LDC.n

        rstate = {"i": 0, "free": [None, None]}
        bstate = {"i": 0, "free": [None, None]}

        def ring_load(src_ap, rows=128, cols=2048):
            i = rstate["i"]
            rstate["i"] += 1
            slot = i % 2
            n = E("sync", lambda q, d=ringF[slot][0:rows, 0:cols], s=src_ap:
                  q.dma_start(d, s), wait=[rstate["free"][slot]], inc=LDS[slot])
            return slot, n

        def ring_free(slot, ctr, val):
            rstate["free"][slot] = (ctr, val)

        def cast_slab(slot, ldn, cols=2048, dst=None):
            if dst is None:
                j = bstate["i"]
                bstate["i"] += 1
                bslot = j % 2
                n = E("gpsimd", lambda g, d=ringB[bslot][:, 0:cols],
                      s=ringF[slot][:, 0:cols]: g.tensor_copy(d, s),
                      wait=[(LDS[slot], ldn), bstate["free"][bslot]], inc=PL)
                ring_free(slot, PL, n)
                return bslot, n
            n = E("gpsimd", lambda g, d=dst, s=ringF[slot][:, 0:cols]:
                  g.tensor_copy(d, s), wait=[(LDS[slot], ldn)], inc=PL)
            ring_free(slot, PL, n)
            return None, n

        def ringB_free(bslot, ctr, val):
            bstate["free"][bslot] = (ctr, val)

        gslot, gn = ring_load(dgate, cols=NKT * 16)
        _, gcast = cast_slab(gslot, gn, cols=NKT * 16, dst=gate_b[:, :])

        E("scalar", lambda s_: s_.activation(aneg[:, :], alog[:, :], AF.Exp),
          wait=[(LDC, CONST_LD)], inc=A)
        E("vector", lambda v: v.tensor_scalar(aneg[:, :], aneg[:, :], -1.0, None,
                                              ALU.mult), wait=[(A, A.n)], inc=V)
        ANEG_V = V.n

        st8["gchunk"] = 0
        st8["pbt_evac"] = None

        def do_slice(s):
            own = (s == NSL - 1)
            row0 = 1024 * s
            bar = [(T, T.n), (V, V.n), (A, A.n), (PL, PL.n)]
            for engname in ("tensor", "vector", "scalar", "gpsimd"):
                E(engname, None, wait=bar)

            tr_done = {}
            for tt in range(9):
                rows = 128 if tt < 8 else 3
                r0 = row0 + 128 * tt
                slot, ldn = ring_load(dx[r0:r0 + rows, :], rows=rows)
                xt = ringF[slot][0:rows, 0:2048]
                E("vector", lambda v, xt=xt, rows=rows, slot=slot:
                  v.reduce_sum(lns[0:rows, 0:1], xt, AX.X),
                  wait=[(LDS[slot], ldn)], inc=V)
                E("vector", lambda v, rows=rows: v.tensor_scalar(
                    lns[0:rows, 1:2], lns[0:rows, 0:1], 1.0 / 2048, None, ALU.mult),
                  wait=[(V, V.n)], inc=V)
                sqw = [(V, V.n)]
                if st8.get("ln_sq_V"):
                    sqw.append((V, st8["ln_sq_V"]))
                E("vector", lambda v, xt=xt, rows=rows:
                  v.tensor_tensor(sqln[0:rows, :], xt, xt, ALU.mult),
                  wait=sqw, inc=V)
                E("vector", lambda v, rows=rows:
                  v.reduce_sum(lns[0:rows, 2:3], sqln[0:rows, :], AX.X),
                  wait=[(V, V.n)], inc=V)
                st8["ln_sq_V"] = V.n
                E("vector", lambda v, rows=rows: v.tensor_scalar(
                    lns[0:rows, 2:3], lns[0:rows, 2:3], 1.0 / 2048, None, ALU.mult),
                  wait=[(V, V.n)], inc=V)
                E("vector", lambda v, rows=rows: v.scalar_tensor_tensor(
                    lns[0:rows, 3:4], lns[0:rows, 1:2], lns[0:rows, 1:2],
                    lns[0:rows, 2:3], ALU.mult, ALU.subtract),
                  wait=[(V, V.n)], inc=V)
                E("scalar", lambda s_, rows=rows: s_.activation(
                    lns[0:rows, 4:5], lns[0:rows, 3:4], AF.Sqrt,
                    bias=consts[0:rows, 1:2], scale=-1.0), wait=[(V, V.n)], inc=A)
                E("vector", lambda v, rows=rows: v.reciprocal(
                    lns[0:rows, 5:6], lns[0:rows, 4:5]), wait=[(A, A.n)], inc=V)
                E("vector", lambda v, rows=rows, s=s, tt=tt: v.tensor_tensor(
                    lns[0:rows, 5:6], lns[0:rows, 5:6],
                    mask_ln[0:rows, s * 9 + tt:s * 9 + tt + 1], ALU.mult),
                  wait=[(V, V.n)], inc=V)
                xb = xn_tm[tt % 2]
                applyw = [(T, tr_done[tt - 2])] if (tt - 2) in tr_done else []
                napply = E("vector", lambda v, xt=xt, xb=xb, rows=rows:
                           v.tensor_scalar(xb[0:rows, 0:2048], xt, lns[0:rows, 1:2],
                                           lns[0:rows, 5:6], ALU.subtract, ALU.mult),
                           wait=applyw + [(V, V.n)], inc=V)
                ring_free(slot, V, napply)
                for g2 in range(2):
                    tw = [(V, napply), (LDC, CONST_LD), st8["pbt_evac"]]
                    for u in range(8):
                        kt = g2 * 8 + u
                        E("tensor", lambda t, xb=xb, kt=kt, u=u, rows=rows:
                          t.transpose(pbt[:, 128 * u:128 * u + rows],
                                      xb[0:rows, 128 * kt:128 * (kt + 1)],
                                      identb[0:rows, 0:rows]),
                          wait=tw if u == 0 else [], inc=T if u == 7 else None)
                    tn = T.n
                    tr_done[tt] = tn
                    for u in range(8):
                        kt = g2 * 8 + u
                        col = 128 * tt
                        E("scalar", lambda s_, kt=kt, u=u, col=col, rows=rows:
                          s_.copy(xn_fm[:, kt * WIN + col:kt * WIN + col + rows],
                                  pbt[:, 128 * u:128 * u + rows]),
                          wait=[(T, tn)] if u == 0 else [],
                          inc=A if u == 7 else None)
                    st8["pbt_evac"] = (A, A.n)
            XN_A = A.n
            if own:
                st8["xn_A_own"] = XN_A
                if "xn" in ddump:
                    E("scalar", lambda q: q.dma_start(ddump["xn"], xn_fm[:, :]),
                      wait=[(A, XN_A)], inc=SC)

            if cfg.get("stop", "all") == "ln":
                return
            tiles = list(range(32, 64)) + [TILE_B] + ([TILE_C] if own else []) \
                + [TILE_DT]
            evac_hist = []
            conv_hist = []
            for ti, mt in enumerate(tiles):
                slot, ldn = ring_load(dwin[mt])
                bslot, cvn = cast_slab(slot, ldn)
                wsl = ringB[bslot][:, 0:2048].rearrange("p (k m) -> p k m", k=NKT)
                bA = pb[0 + 2 * (ti % 2)]
                bB = pb[1 + 2 * (ti % 2)]
                bC = pb[4 + (ti % 2)]
                mmw = [(A, XN_A), (PL, cvn)]
                if len(evac_hist) >= 2:
                    mmw.append((A, evac_hist[-2]))
                elif st8.get("prev_slice_tail"):
                    mmw.append(st8["prev_slice_tail"])
                lastmm = None
                for k in range(NKT):
                    E("tensor", lambda t, wsl=wsl, k=k, bA=bA:
                      t.matmul(bA[:, :], wsl[:, k, :], xn3[:, k, 0:512],
                               start=(k == 0), stop=(k == NKT - 1)),
                      wait=mmw if k == 0 else [])
                    E("tensor", lambda t, wsl=wsl, k=k, bB=bB:
                      t.matmul(bB[:, :], wsl[:, k, :], xn3[:, k, 512:1024],
                               start=(k == 0), stop=(k == NKT - 1)))
                    lastmm = E("tensor", lambda t, wsl=wsl, k=k, bC=bC:
                               t.matmul(bC[:, 0:3], wsl[:, k, :], xn3[:, k, 1024:1027],
                                        start=(k == 0), stop=(k == NKT - 1)),
                               inc=T if k == NKT - 1 else None)
                ringB_free(bslot, T, lastmm)
                evw = [(T, lastmm)]
                if conv_hist:
                    evw.append((V, conv_hist[-1]))
                E("scalar", lambda s_, bA=bA: s_.copy(cs[:, 0:512], bA[:, :]), wait=evw)
                E("scalar", lambda s_, bB=bB: s_.copy(cs[:, 512:1024], bB[:, :]))
                nev = E("scalar", lambda s_, bC=bC: s_.copy(cs[:, 1024:1027],
                                                            bC[:, 0:3]), inc=A)
                evac_hist.append(nev)

                if mt == TILE_DT:
                    d0 = co[0:64, 0:1024]
                    d1 = cs[0:64, 0:1024]
                    E("vector", lambda v, d0=d0: v.tensor_scalar(
                        d0, cs[0:64, 3:1027], dtb[:, 0:1], None, ALU.add),
                      wait=[(A, nev)], inc=V)
                    E("scalar", lambda s_, d0=d0, d1=d1: s_.activation(
                        d1, d0, AF.Abs), wait=[(V, V.n)], inc=A)
                    E("scalar", lambda s_, d1=d1: s_.activation(
                        d1, d1, AF.Exp, scale=-1.0), inc=A)
                    E("scalar", lambda s_, d1=d1: s_.activation(
                        d1, d1, AF.Ln, bias=consts[0:64, 0:1]), inc=A)
                    E("vector", lambda v, d0=d0: v.tensor_scalar(
                        d0, d0, 0.0, None, ALU.max), wait=[(A, A.n)], inc=V)
                    dtw = [(T, st8["acs_T"])] if st8.get("acs_T") else []
                    E("vector", lambda v, d0=d0, d1=d1: v.tensor_tensor(
                        dt_sl, d0, d1, ALU.add), wait=dtw, inc=V)
                    E("vector", lambda v: v.tensor_scalar(
                        dt_sl, dt_sl, 100.0, None, ALU.min), inc=V)
                    nv = E("vector", lambda v: v.tensor_scalar(
                        dA_sl, dt_sl, aneg[:, 0:1], None, ALU.mult),
                        wait=[(V, ANEG_V)], inc=V)
                    conv_hist.append(nv)
                else:
                    j = mt - 32
                    cw0 = [(A, nev)]
                    E("vector", lambda v, j=j: v.tensor_scalar(
                        co[:, :], cs[:, 0:1024], convw[:, 4 * j:4 * j + 1], None,
                        ALU.mult), wait=cw0, inc=V)
                    for k in range(1, 4):
                        E("vector", lambda v, j=j, k=k: v.scalar_tensor_tensor(
                            co[:, :], cs[:, k:1024 + k],
                            convw[:, 4 * j + k:4 * j + k + 1], co[:, :],
                            ALU.mult, ALU.add), inc=V if k == 3 else None)
                    nconv = V.n
                    conv_hist.append(nconv)
                    if mt < 64:
                        dst = big1[:, (mt - 32) * TOK:(mt - 31) * TOK]
                        E("scalar", lambda s_, dst=dst, j=j: s_.activation(
                            dst, co[:, :], AF.Silu, bias=convb[:, j:j + 1]),
                          wait=[(V, nconv)], inc=A)
                    else:
                        dst = Bfm[:, :] if mt == TILE_B else Cfm[:, :]
                        E("scalar", lambda s_, j=j: s_.activation(
                            cs[:, 0:1024], co[:, :], AF.Silu,
                            bias=convb[:, j:j + 1]),
                          wait=[(V, nconv)], inc=A)
                        nbc_ = E("vector", lambda v, dst=dst: v.tensor_copy(
                            dst, cs[:, 0:1024]), wait=[(A, A.n)], inc=V)
                        conv_hist.append(nbc_)
            CONV_A = A.n
            CONV_V = V.n
            st8["prev_slice_tail"] = (A, CONV_A)

            if own:
                for tag, buf, wc in [("xfm", big1[:, :], (A, CONV_A)),
                                     ("bfm", Bfm[:, :], (A, CONV_A)),
                                     ("cfm", Cfm[:, :], (A, CONV_A)),
                                     ("dt", dt_sl, (V, CONV_V))]:
                    if tag in ddump:
                        E("scalar", lambda q, b=buf, t_=tag: q.dma_start(
                            ddump[t_], b), wait=[wc], inc=SC)

            stop_ = cfg.get("stop", "all")
            if stop_ == "ln2":
                return
            if stop_ != "inproj":
                nch = {"chunk1": 1, "chunk2": 2, "ssm_tr": 1, "ssm_xtm": 1,
                       "ssm_gt": 1, "ssm_z": 1, "ssm_brow": 1, "ssm_yd": 1}.get(
                           stop_, 4)
                for c in range(nch):
                    do_chunk(s, c, own, CONV_A, CONV_V)

            if own and "y" in ddump:
                E("scalar", lambda q: q.dma_start(ddump["y"], big1[:, :]),
                  wait=[(V, V.n)], inc=SC)
            if own and "r" in ddump:
                E("scalar", lambda q: q.dma_start(ddump["r"], R),
                  wait=[(V, V.n)], inc=SC)

        def do_chunk(s, c, own, CONV_A, CONV_V):
            g = st8["gchunk"]
            st8["gchunk"] += 1
            col0 = 256 * c

            scanw = [(V, CONV_V)]
            if st8.get("acs_T"):
                scanw.append((T, st8["acs_T"]))
            if st8.get("acs_SC"):
                scanw.append((SCA, st8["acs_SC"]))
            E("vector", lambda v, col0=col0: v.tensor_tensor_scan(
                acs[:, :], dA_sl[:, col0:col0 + 256],
                consts[0:64, 2:3].broadcast_to([64, 256]), 0.0, ALU.add, ALU.add),
              wait=scanw, inc=V)
            nscan = V.n

            trw = [(V, nscan), st8.get("pb0_free")]
            for lb in range(2):
                E("tensor", lambda t, lb=lb: t.transpose(
                    pb[0][:, 64 * lb:64 * (lb + 1)], acs[:, 128 * lb:128 * (lb + 1)],
                    identf[0:64, 0:64]), wait=trw if lb == 0 else [])
            for lb in range(2):
                E("tensor", lambda t, lb=lb, col0=col0: t.transpose(
                    pb[0][:, 128 + 64 * lb:128 + 64 * (lb + 1)],
                    dt_sl[:, col0 + 128 * lb:col0 + 128 * (lb + 1)],
                    identf[0:64, 0:64]))
            E("tensor", lambda t: t.transpose(
                pb[0][0:1, 256:320], acs[:, 255:256], identf[0:64, 0:64]), inc=T)
            ntr = T.n
            st8["acs_T"] = ntr
            E("scalar", lambda s_: s_.copy(acsT[:, :], pb[0][:, 0:128]),
              wait=[(T, ntr)])
            E("scalar", lambda s_: s_.copy(dtT[:, :], pb[0][:, 128:256]))
            E("scalar", lambda s_: s_.copy(erow, pb[0][0:1, 256:320]), inc=A)
            nevt = A.n
            st8["pb0_free"] = (A, nevt)

            ebcw = [(A, nevt), st8.get("pb1_free")]
            E("tensor", lambda t: t.matmul(pb[1][:, 64:128], ones_f[0:1, :], erow,
                                           start=True, stop=True), wait=ebcw, inc=T)
            nbc = T.n
            E("vector", lambda v: v.tensor_tensor(
                wt_sb[:, :].rearrange("p (l h) -> p l h", l=2),
                pb[1][:, 64:128].unsqueeze(1).broadcast_to([128, 2, 64]),
                acsT3[:, :, :], ALU.subtract), wait=[(T, nbc), (A, nevt)], inc=V)
            nwt0 = V.n
            E("scalar", lambda s_: s_.activation(wt_sb[:, :], wt_sb[:, :], AF.Exp),
              wait=[(V, nwt0)], inc=A)
            E("vector", lambda v: v.tensor_tensor(wt_sb[:, :], wt_sb[:, :],
                                                  dtT[:, :], ALU.mult),
              wait=[(A, A.n)], inc=V)
            NWT = V.n
            nexp_e = E("scalar", lambda s_: s_.activation(erow, erow, AF.Exp),
                       wait=[(T, nbc)], inc=A)
            E("tensor", lambda t: t.matmul(pb[1][:, 0:64], ones_f[0:1, :], erow,
                                           start=True, stop=True),
              wait=[(A, nexp_e), (V, nwt0)], inc=T)
            ndecT = T.n
            ndec = E("scalar", lambda s_: s_.copy(dec_sb[:, :], pb[1][:, 0:64]),
                     wait=[(T, ndecT)], inc=A)
            st8["pb1_free"] = (A, ndec)

            if cfg.get("stop") == "ssm_tr":
                return
            for grp in range(8):
                tw = [(A, CONV_A), st8["pbt_evac"]]
                if st8.get("xtm_war"):
                    tw.append(st8["xtm_war"])
                for u in range(8):
                    idx = grp * 8 + u
                    mt, lb = idx // 2, idx % 2
                    E("tensor", lambda t, mt=mt, lb=lb, u=u, col0=col0:
                      t.transpose(pbt[:, 128 * u:128 * (u + 1)],
                                  big1[:, mt * TOK + col0 + 128 * lb:
                                       mt * TOK + col0 + 128 * lb + 128],
                                  identb[:, :]),
                      wait=tw if u == 0 else [], inc=T if u == 7 else None)
                tn = T.n
                for u in range(8):
                    idx = grp * 8 + u
                    mt, lb = idx // 2, idx % 2
                    E("vector", lambda v, mt=mt, lb=lb, u=u:
                      v.tensor_copy(x_tm[lb][:, 128 * mt:128 * (mt + 1)],
                                    pbt[:, 128 * u:128 * (u + 1)]),
                      wait=[(T, tn)] if u == 0 else [], inc=V if u == 7 else None)
                st8["pbt_evac"] = (V, prog.ctrs["V"].n)
            XTM_A = (V, prog.ctrs["V"].n)

            if cfg.get("stop") == "ssm_xtm":
                return
            if own:
                do_own_body(s, c, col0, XTM_A, NWT, nevt, g, CONV_A)
                if cfg.get("stop") in ("ssm_gt", "ssm_z", "ssm_brow", "ssm_yd"):
                    return
            st8["xtm_war"] = None

            mb = s * 8 + 2 * c
            xdw = [XTM_A, (V, NWT), (LDC, CONST_LD)]
            if own and st8.get("yoff_T"):
                xdw.append((T, st8["yoff_T"]))
            for lb in range(2):
                E("vector", lambda v, lb=lb, mb=mb: v.scalar_tensor_tensor(
                    x_tm[lb][:, :].rearrange("p (h q) -> p h q", h=64),
                    x_tm[lb][:, :].rearrange("p (h q) -> p h q", h=64),
                    mask_blk[:, mb + lb:mb + lb + 1],
                    wt_sb[:, :].rearrange("p (l h) -> p l h", l=2)[:, lb, :]
                    .unsqueeze(2).broadcast_to([128, 64, 64]),
                    ALU.mult, ALU.mult),
                  wait=xdw if lb == 0 else [], inc=V)
            NXDW = V.n
            st8["xtm_war"] = (V, NXDW)

            btw = [(A, CONV_A), st8["pbt_evac"]]
            for lb in range(2):
                E("tensor", lambda t, lb=lb, col0=col0: t.transpose(
                    pbt[:, 128 * lb:128 * (lb + 1)],
                    Bfm[:, col0 + 128 * lb:col0 + 128 * (lb + 1)], identb[:, :]),
                  wait=btw if lb == 0 else [], inc=T if lb == 1 else None)
            nbt = E("vector", lambda v: v.tensor_copy(B_tm[:, :], pbt[:, 0:256]),
                    wait=[(T, T.n)], inc=V)
            st8["pbt_evac"] = (V, nbt)

            for rnd in range(4):
                for i in range(2):
                    gt = rnd * 2 + i
                    bank = pb[4 + i]
                    smw = [(V, nbt), (V, NXDW), st8.get(f"st_free{i}")]
                    if own and rnd == 0:
                        smw += [(V, V.n), (A, A.n)]
                    for lb in range(2):
                        E("tensor", lambda t, gt=gt, lb=lb, bank=bank:
                          t.matmul(bank[:, :],
                                   B_tm[:, :].rearrange("p (l n) -> p l n", l=2)
                                   [:, lb, :],
                                   x_tm[lb][:, 512 * gt:512 * (gt + 1)],
                                   start=(lb == 0), stop=(lb == 1)),
                          wait=smw if lb == 0 else [],
                          inc=T if lb == 1 else None)
                    stn = T.n
                    rw = [(T, stn), (A, ndec)]
                    if g == 0:
                        nru = E("vector", lambda v, gt=gt, bank=bank: v.tensor_copy(
                            R[:, 512 * gt:512 * (gt + 1)], bank[:, :]),
                            wait=rw, inc=V)
                    else:
                        E("vector", lambda v, gt=gt: v.tensor_tensor(
                            R[:, 512 * gt:512 * (gt + 1)].rearrange(
                                "p (h q) -> p h q", h=8),
                            R[:, 512 * gt:512 * (gt + 1)].rearrange(
                                "p (h q) -> p h q", h=8),
                            dec_sb[:, 8 * gt:8 * (gt + 1)].unsqueeze(2)
                            .broadcast_to([128, 8, 64]), ALU.mult),
                          wait=rw, inc=V)
                        nru = E("vector", lambda v, gt=gt, bank=bank:
                                v.tensor_tensor(R[:, 512 * gt:512 * (gt + 1)],
                                                R[:, 512 * gt:512 * (gt + 1)],
                                                bank[:, :], ALU.add), inc=V)
                    st8[f"st_free{i}"] = (V, nru)
            st8["R_V"] = V.n

        def do_own_body(s, c, col0, XTM_A, NWT, nevt, g, CONV_A):
            if g == 0:
                nrb = E("vector", lambda v: v.tensor_scalar(
                    Rb[:, :], big1[:, 0:4096], 0.0, None, ALU.mult),
                    wait=[(V, st8.get("R_V", 0)), (A, CONV_A)], inc=V)
            else:
                nrb = E("vector", lambda v: v.tensor_copy(Rb[:, :], R[:, :]),
                        wait=[(V, st8["R_V"])], inc=V)
            NRB = nrb

            gtw = [(A, CONV_A), XTM_A,
                   st8.get("st_free0"), st8.get("st_free1")]
            E("tensor", lambda t, col0=col0: t.matmul(
                pb[4][:, 0:256], Bfm[:, col0:col0 + 128], Cfm[:, col0:col0 + 256],
                start=True, stop=True), wait=gtw)
            E("tensor", lambda t, col0=col0: t.matmul(
                pb[5][:, 0:128], Bfm[:, col0 + 128:col0 + 256],
                Cfm[:, col0 + 128:col0 + 256], start=True, stop=True), inc=T)
            ngt = T.n
            E("vector", lambda v: v.tensor_tensor(GTzb[:, 0:128], pb[4][:, 0:128],
                                                  tri[:, :], ALU.mult),
              wait=[(T, ngt)])
            E("vector", lambda v: v.tensor_copy(GTzb[:, 128:256], pb[4][:, 128:256]))
            E("vector", lambda v: v.tensor_tensor(GTzb[:, 256:384], pb[5][:, 0:128],
                                                  tri[:, :], ALU.mult), inc=V)
            NGTZ = V.n

            acsw = [(A, nevt)]
            if st8.get("flat_done"):
                acsw.append(st8["flat_done"])
            nacs = E("scalar", lambda q: q.dma_start(dacs, acs[:, :]),
                     wait=acsw, inc=SCA)
            st8["acs_SC"] = nacs

            if cfg.get("stop") == "ssm_gt":
                return
            brow_free = {0: [(V, NGTZ)], 1: [(V, NGTZ)]}
            flatn = [None] * 16
            flat_user = [0] * 16
            sd_war = []
            ypair_war = []
            zhist = []

            zmax = cfg.get("z_count", 32)
            fmax = cfg.get("f_count", 32)
            for f in range(32):
                if cfg.get("stop") == "ssm_z" and f >= zmax:
                    break
                if f >= fmax:
                    break
                if f % 2 == 0 and not cfg.get("no_flats"):
                    j = f // 2
                    wj = [(SCA, nacs)]
                    if j >= 2:
                        wj.append((T, flat_user[j - 2]))
                    flatn[j] = E("scalar", lambda q, j=j: q.dma_start(
                        flat4[j % 2], dacs.rearrange("p f -> (p f)").unsqueeze(0)
                        [0:1, 1024 * j:1024 * (j + 1)]), wait=wj, inc=SFS[j % 2])
                    st8["flat_done"] = (SFS[j % 2], flatn[j])

                h0, h1 = 2 * f, 2 * f + 1
                if cfg.get("no_z"):
                    continue
                slot, ldn = ring_load(dwin[f])
                bslot, cvn = cast_slab(slot, ldn)
                wsl = ringB[bslot][:, 0:2048].rearrange("p (k m) -> p k m", k=NKT)
                zbank = pb[1] if f % 2 == 0 else pb[0]
                zw = [(PL, cvn)]
                if f == 0:
                    zw.append((V, NWT))
                if f == 1:
                    zw.append((A, nevt))
                if f >= 2:
                    zw.append((A, zhist[f - 2]))
                lastz = None
                nmm_z = 1 if cfg.get("z_nomm") else NKT
                for k in range(nmm_z):
                    lastz = E("tensor", lambda t, wsl=wsl, k=k, zbank=zbank,
                              col0=col0: t.matmul(
                                  zbank[:, 0:256], wsl[:, k, :],
                                  xn3[:, k, col0 + 3:col0 + 259],
                                  start=(k == 0), stop=(k == nmm_z - 1)),
                              wait=zw if k == 0 else [],
                              inc=T if k == nmm_z - 1 else None)
                ringB_free(bslot, T, lastz)
                silw = [(T, lastz)]
                if f >= 2 and st8.get(f"sq_T{f % 2}"):
                    silw.append((T, st8[f"sq_T{f % 2}"]))
                if st8.get("stg_V"):
                    silw.append((V, st8["stg_V"]))
                nsz = E("scalar", lambda s_, zbank=zbank: s_.activation(
                    stg[:, :], zbank[:, 0:256], AF.Silu), wait=silw, inc=A)
                nszv = E("vector", lambda v, f=f: v.tensor_copy(
                    szp[f % 2][:, :], stg[:, :]), wait=[(A, nsz)], inc=V)
                st8["stg_V"] = nszv
                zhist.append(nsz)

                if cfg.get("stop") == "ssm_z":
                    continue
                if cfg.get("no_z"):
                    st8["yoff_T"] = prog.ctrs["T"].n or 1
                    continue
                for h in (h0, h1):
                    par = h % 2
                    bbank = pb[4 + par]
                    bw = [(SFS[(h // 4) % 2], flatn[h // 4])] + brow_free[par]
                    nb = E("tensor", lambda t, h=h, bbank=bbank: t.matmul(
                        bbank[:, 0:256], ones_f[0:1, :],
                        flat4[(h // 4) % 2][0:1, 256 * (h % 4):256 * (h % 4) + 256],
                        start=True, stop=True), wait=bw, inc=T)
                    flat_user[h // 4] = nb
                    stgw = [(T, nb)] + sd_war
                    if st8.get("stg_V"):
                        stgw.append((V, st8["stg_V"]))
                    nsd = E("scalar", lambda s_, bbank=bbank:
                            s_.activation(stg[:, :], bbank[:, 0:256], AF.Exp),
                            wait=stgw, inc=A)
                    nsdv = E("vector", lambda v, par=par: v.tensor_copy(
                        sd_h[par][:, :], stg[:, :]), wait=[(A, nsd)], inc=V)
                    st8["stg_V"] = nsdv
                    segw = [(T, nb)]
                    if st8.get("segA"):
                        segw.append((A, st8["segA"]))
                    E("vector", lambda v, h=h, bbank=bbank: v.tensor_scalar(
                        seg[:, 0:256], bbank[:, 0:256], acsT3[:, 0, h:h + 1],
                        None, ALU.subtract), wait=segw)
                    nseg = E("vector", lambda v, h=h, bbank=bbank: v.tensor_scalar(
                        seg[:, 256:384], bbank[:, 128:256],
                        acsT3[:, 1, h:h + 1], None, ALU.subtract), inc=V)
                    brow_free[par] = [(V, nseg), (A, nsd)]
                    E("gpsimd", lambda g_: g_.tensor_tensor(
                        seg[:, 0:128], seg[:, 0:128], tri[:, :], ALU.mult),
                      wait=[(V, nseg)])
                    npm = E("gpsimd", lambda g_: g_.tensor_tensor(
                        seg[:, 256:384], seg[:, 256:384], tri[:, :], ALU.mult),
                        inc=PL)
                    nexp = E("scalar", lambda s_: s_.activation(
                        seg[:, :], seg[:, :], AF.Exp), wait=[(PL, npm)], inc=A)
                    st8["segA"] = nexp
                    mtw = [(A, nexp)]
                    if f == 0 and h == h0:
                        mtw.append((V, NGTZ))
                    E("vector", lambda v, h=h, par=par: v.tensor_scalar(
                        Mt[par][:, 0:256], GTzb[:, 0:256], dtT3[:, 0, h:h + 1],
                        None, ALU.mult), wait=mtw)
                    E("vector", lambda v, h=h, par=par: v.tensor_scalar(
                        Mt[par][:, 256:384], GTzb[:, 256:384], dtT3[:, 1, h:h + 1],
                        None, ALU.mult))
                    nmt = E("vector", lambda v, par=par: v.tensor_tensor(
                        Mt[par][:, :], Mt[par][:, :], seg[:, :], ALU.mult),
                        inc=V)
                    if cfg.get("stop") == "ssm_brow":
                        st8["yoff_T"] = nb
                        continue
                    ydw = [(V, nmt)] + ypair_war
                    noyo = cfg.get("no_yo")
                    noyd = cfg.get("no_yd")
                    nyo = nb
                    if not noyd:
                        E("tensor", lambda t, h=h, par=par: t.matmul(
                            pb[2][64 * par:64 * par + 64, 0:256],
                            x_tm[0][:, 64 * h:64 * h + 64], Mt[par][:, 0:256],
                            start=True, stop=False), wait=ydw)
                        nyo = E("tensor", lambda t, h=h, par=par: t.matmul(
                            pb[2][64 * par:64 * par + 64, 128:256],
                            x_tm[1][:, 64 * h:64 * h + 64], Mt[par][:, 256:384],
                            start=False, stop=True), inc=T if noyo else None)
                    if not noyo:
                        yow = [(V, NRB)] + ypair_war + (ydw if noyd else [])
                        ysub = cfg.get("yo_sub", 0)
                        lhs_ap = big1[:, 0:64] if ysub == 1 else Rb[:, 64 * h:64 * h + 64]
                        rhs_ap = (big1[:, 1024:1280] if ysub == 2
                                  else Cfm[:, col0:col0 + 256])
                        out_ap = (pb[3][0:64, 0:256] if ysub == 3
                                  else pb[3][64 * par:64 * par + 64, 0:256])
                        if ysub == 4:
                            yow = [(PL, nmt)]
                        if ysub == 5:
                            lhs_ap = Rb[:, 0:128]
                            out_ap = pb[3][:, 0:256]
                        if ysub == 6:
                            E("tensor", lambda t: t.drain(), wait=[])
                        if ysub == 7:
                            out_ap = pb[6][0:64, 0:256]
                        if ysub == 8:
                            yow = []
                        if ysub == 9:
                            yow = list(ypair_war)
                        if ysub == 10:
                            yow = [(V, NRB)]
                        if ysub == 11:
                            yow = [(V, nmt)] if noyd else yow
                        if ysub == 12:
                            yow = [(A, nexp)]
                        if ysub == 13:
                            yow = [(PL, npm)]
                        if ysub == 14:
                            yow = [(V, nseg)]
                        if ysub == 15:
                            yow = [(A, nsd)]
                        nyo = E("tensor", lambda t, la=lhs_ap, ra=rhs_ap, oa=out_ap:
                                t.matmul(oa, la, ra, start=True, stop=True),
                                wait=yow, inc=T)
                        if ysub == 6:
                            E("tensor", lambda t: t.drain(), wait=[])
                st8["yoff_T"] = nyo

                if cfg.get("stop") in ("ssm_z", "ssm_brow"):
                    continue
                pen = cfg.get("pe_n", 6)
                if pen < 0:
                    nyf0 = E("vector", lambda v: v.tensor_scalar(
                        yf_sb, yo_sb, 1.0, None, ALU.mult), wait=[(T, nyo)], inc=V)
                    ypair_war = [(V, nyf0)]
                    sd_war = [(V, nyf0)]
                    continue
                pw = [(T, nyo), (A, zhist[f])]
                if pen >= 1:
                    E("vector", lambda v: v.tensor_tensor(
                        yo_sb[0:64, :], pb[3][0:64, 0:256], sd_h[0][0:64, :],
                        ALU.mult), wait=pw)
                else:
                    E("vector", lambda v: v.tensor_copy(yo_sb, pb[3][:, 0:256]),
                      wait=pw)
                if pen >= 2:
                    E("vector", lambda v: v.tensor_tensor(
                        yo_sb[64:128, :], pb[3][64:128, 0:256],
                        sd_h[1][64:128, :], ALU.mult))
                if pen >= 3:
                    E("vector", lambda v, f=f, col0=col0: v.scalar_tensor_tensor(
                        yo_sb, big1[:, f * TOK + col0:f * TOK + col0 + 256],
                        dexp[:, f:f + 1], yo_sb, ALU.mult, ALU.add))
                if pen >= 4:
                    nyf0 = E("vector", lambda v: v.scalar_tensor_tensor(
                        yf_sb, pb[2][:, 0:256], 1.0, yo_sb, ALU.mult, ALU.add),
                        inc=V)
                else:
                    nyf0 = E("vector", lambda v: v.tensor_copy(
                        yf_sb, pb[2][:, 0:256]), inc=V)
                ypair_war = [(V, nyf0)]
                sd_war = [(V, nyf0)]
                if pen >= 5:
                    nyf = E("vector", lambda v, f=f: v.tensor_tensor(
                        yf_sb, yf_sb, szp[f % 2][:, :], ALU.mult), inc=V)
                else:
                    nyf = nyf0
                if not cfg.get("yd_novar"):
                    sqw = [(V, nyf)]
                    if f >= 2:
                        sqw.append((T, st8[f"sq_T{f % 2}"]))
                    nsq = E("vector", lambda v, f=f: v.tensor_tensor(
                        sq_sb[f % 2][:, :], yf_sb, yf_sb, ALU.mult),
                        wait=sqw, inc=V)
                    nvm = E("tensor", lambda t, f=f: t.matmul(
                        pb[6][0:1, 0:256], ones_b[:, :], sq_sb[f % 2][:, :],
                        start=(f == 0), stop=(f == 31)), wait=[(V, nsq)], inc=T)
                    st8[f"sq_T{f % 2}"] = nvm
                E("vector", lambda v, f=f, col0=col0: v.tensor_copy(
                    big1[:, f * TOK + col0:f * TOK + col0 + 256], yf_sb), inc=V)

            if len(zhist) >= 32:
                st8["pb0_free"] = (A, zhist[31])
                st8["pb1_free"] = (A, zhist[30])
            elif zhist:
                st8["pb0_free"] = (A, zhist[-1])
                st8["pb1_free"] = (A, zhist[-1])
            if (cfg.get("stop") in ("ssm_z", "ssm_brow") or cfg.get("yd_novar")
                    or fmax < 32):
                st8["yn_V"] = prog.ctrs["V"].n
                return
            nvarT = max(st8["sq_T0"], st8["sq_T1"])
            E("scalar", lambda s_: s_.activation(vrow, pb[6][0:1, 0:256], AF.Copy,
                                                 scale=1.0 / 4096),
              wait=[(T, nvarT)])
            E("scalar", lambda s_: s_.activation(vrow, vrow, AF.Sqrt,
                                                 bias=consts[0:1, 1:2]), inc=A)
            nrstA = A.n
            E("vector", lambda v: v.reciprocal(vrow, vrow), wait=[(A, nrstA)], inc=V)
            nrst = V.n
            E("tensor", lambda t: t.matmul(pb[6][:, 256:512], ones_f[0:1, :], vrow,
                                           start=True, stop=True),
              wait=[(V, nrst)], inc=T)
            nrb2 = T.n
            for f in range(32):
                E("vector", lambda v, f=f, col0=col0: v.scalar_tensor_tensor(
                    big1[:, f * TOK + col0:f * TOK + col0 + 256],
                    big1[:, f * TOK + col0:f * TOK + col0 + 256],
                    gam[:, f:f + 1], pb[6][:, 256:512], ALU.mult, ALU.mult),
                  wait=[(T, nrb2)] if f == 0 else [], inc=V if f == 31 else None)
            st8["yn_V"] = V.n

        def do_outproj():
            YN = st8["yn_V"]
            for tq in range(4):
                evs = st8.get("op_evac")
                for kt in range(32):
                    slot, ldn = ring_load(dwout[kt])
                    bslot, cvn = cast_slab(slot, ldn)
                    wsl = ringB[bslot][:, 0:2048]
                    mw = [(PL, cvn), (V, YN), (V, st8["R_V"]),
                          (A, prog.ctrs["Aa"].n), (PL, prog.ctrs["PL"].n)]
                    if evs:
                        mw.append((V, evs))
                    lastm = None
                    for o in range(16):
                        bank, off = dbank(o)
                        lastm = E("tensor", lambda t, wsl=wsl, o=o, kt=kt, tq=tq,
                                  bank=bank, off=off: t.matmul(
                                      bank[:, off:off + 256],
                                      wsl[:, 128 * o:128 * (o + 1)],
                                      big1[:, kt * TOK + 256 * tq:
                                           kt * TOK + 256 * (tq + 1)],
                                      start=(kt == 0 and o % 2 == 0),
                                      stop=(kt == 31), skip_group_check=True),
                                  wait=mw if o == 0 else [],
                                  inc=T if (o == 15) else None)
                    ringB_free(bslot, T, lastm)
                nmm = T.n
                for o in range(16):
                    bank, off = dbank(o)
                    E("vector", lambda v, o=o, tq=tq, bank=bank, off=off:
                      v.tensor_copy(acc_view(o, tq), bank[:, off:off + 256]),
                      wait=[(T, nmm)] if o == 0 else [], inc=V if o == 15 else None)
                st8["op_evac"] = V.n
            st8["dn_ev_V"] = st8["op_evac"]

        def do_gating():
            XN = st8["xn_A_own"]
            for tb in range(8):
                gw = [(A, XN), (PL, gcast), (V, st8["op_evac"])]
                if st8.get("gate_evac"):
                    gw += [(A, st8["gate_evac"]), (V, st8["cw_V"])]
                lastg = None
                for k in range(NKT):
                    lastg = E("tensor", lambda t, k=k, tb=tb: t.matmul(
                        pb[0][:, 0:16],
                        xn3[:, k, 3 + 128 * tb:3 + 128 * (tb + 1)],
                        gate_b[:, :].rearrange("p (k e) -> p k e", k=NKT)[:, k, :],
                        start=(k == 0), stop=(k == NKT - 1)),
                        wait=gw if k == 0 else [], inc=T if k == NKT - 1 else None)
                E("scalar", lambda s_: s_.copy(ga[:, :], pb[0][:, 0:16]),
                  wait=[(T, lastg)], inc=A)
                st8["gate_evac"] = A.n
                def EF(fn, extra=None):
                    E("vector", fn, wait=[(V, V.n)] + (extra or []), inc=V)
                E("vector", lambda v: v.reduce_max(gb[:, 0:1], ga[:, :], AX.X),
                  wait=[(A, A.n), (V, V.n)], inc=V)
                EF(lambda v: v.tensor_scalar(gb[:, 0:1], gb[:, 0:1], -1.0,
                                             None, ALU.mult))
                E("scalar", lambda s_: s_.activation(ga[:, :], ga[:, :], AF.Exp,
                                                     bias=gb[:, 0:1]),
                  wait=[(V, V.n)], inc=A)
                E("vector", lambda v: v.reduce_sum(gb[:, 1:2], ga[:, :], AX.X),
                  wait=[(A, A.n), (V, V.n)], inc=V)
                EF(lambda v: v.reciprocal(gb[:, 2:3], gb[:, 1:2]))
                EF(lambda v: v.tensor_scalar(ga[:, :], ga[:, :],
                                             gb[:, 2:3], None, ALU.mult))
                EF(lambda v: v.tensor_reduce(
                    gmx[:, :].rearrange("p (a b) -> p a b", b=1),
                    ga[:, :].rearrange("p (a b) -> p a b", b=4), AX.X, ALU.max))
                for it in range(2):
                    EF(lambda v: v.reduce_max(gb[:, 3:4], gmx[:, :], AX.X))
                    EF(lambda v: v.tensor_scalar(
                        wrk[:, 0:4], gmx[:, :], gb[:, 3:4], None, ALU.is_equal))
                    if it == 0:
                        EF(lambda v: v.tensor_copy(gmk[:, :], wrk[:, 0:4]))
                    else:
                        EF(lambda v: v.tensor_tensor(
                            gmk[:, :], gmk[:, :], wrk[:, 0:4], ALU.max))
                    EF(lambda v: v.scalar_tensor_tensor(
                        gmx[:, :], wrk[:, 0:4], -1e30, gmx[:, :], ALU.mult,
                        ALU.add))
                EF(lambda v: v.tensor_tensor(
                    gb[:, 0:16].rearrange("p (a b) -> p a b", b=4),
                    ga[:, :].rearrange("p (a b) -> p a b", b=4),
                    gmk[:, :].unsqueeze(2).broadcast_to([128, 4, 4]), ALU.mult))
                for it in range(4):
                    EF(lambda v: v.reduce_max(gmx[:, 0:1], gb[:, 0:16], AX.X))
                    EF(lambda v: v.tensor_scalar(
                        wrk[:, :], gb[:, 0:16], gmx[:, 0:1], None, ALU.is_equal))
                    if it == 0:
                        EF(lambda v: v.tensor_copy(sel[:, :], wrk[:, :]))
                    else:
                        EF(lambda v: v.tensor_tensor(
                            sel[:, :], sel[:, :], wrk[:, :], ALU.max))
                    EF(lambda v: v.scalar_tensor_tensor(
                        gb[:, 0:16], wrk[:, :], -1e30, gb[:, 0:16], ALU.mult,
                        ALU.add))
                EF(lambda v: v.tensor_tensor(wrk[:, :], sel[:, :],
                                             ga[:, :], ALU.mult))
                EF(lambda v: v.reduce_sum(gmx[:, 0:1], wrk[:, :], AX.X))
                EF(lambda v: v.tensor_scalar(gmx[:, 0:1], gmx[:, 0:1],
                                             1e-20, None, ALU.add))
                EF(lambda v: v.reciprocal(gmx[:, 1:2], gmx[:, 0:1]))
                EF(lambda v, tb=tb: v.tensor_scalar(
                    cw[:, 16 * tb:16 * (tb + 1)], wrk[:, :], gmx[:, 1:2], None,
                    ALU.mult))
                st8["cw_V"] = V.n
            for tb in range(8):
                tw = [(V, st8["cw_V"])]
                if tb == 0:
                    tw.append((A, st8["gate_evac"]))
                else:
                    tw.append((A, st8["cwt_evac"]))
                ntc = E("tensor", lambda t, tb=tb: t.transpose(
                    pb[0][0:16, 0:128], cw[:, 16 * tb:16 * (tb + 1)],
                    identf[:, :]), wait=tw, inc=T)
                E("scalar", lambda s_, tb=tb: s_.copy(
                    cwT[:, 128 * tb:128 * (tb + 1)], pb[0][0:16, 0:128]),
                  wait=[(T, ntc)], inc=A)
                st8["cwt_evac"] = A.n
            E("scalar", lambda q: q.dma_start(dcw, cwT[:, :]),
              wait=[(A, st8["cwt_evac"])], inc=SCW)
            st8["cw_SC"] = SCW.n
            if "cw" in ddump:
                E("scalar", lambda q: q.dma_start(ddump["cw"], cw[:, :]),
                  wait=[(V, st8["cw_V"])], inc=SC)

        def up_tiles(dsrc, ntiles, napairs, e=None):
            XN = st8["xn_A_own"]
            hist = []
            for mt in range(ntiles):
                src = dsrc[e, mt] if e is not None else dsrc[mt]
                slot, ldn = ring_load(src)
                bslot, cvn = cast_slab(slot, ldn)
                wsl = ringB[bslot][:, 0:2048].rearrange("p (k m) -> p k m", k=NKT)
                b1 = pb[0 + 2 * (mt % 2)]
                b2 = pb[1 + 2 * (mt % 2)]
                mw = [(PL, cvn), (A, XN)]
                if len(hist) >= 2:
                    mw.append(hist[-2])
                if mt < 4:
                    mw.append((V, st8["dn_ev_V"]))
                    if st8.get("cwt_evac"):
                        mw.append((A, st8["cwt_evac"]))
                lastm = None
                for k in range(NKT):
                    E("tensor", lambda t, wsl=wsl, k=k, b1=b1: t.matmul(
                        b1[:, :], wsl[:, k, :], xn3[:, k, 3:515],
                        start=(k == 0), stop=(k == NKT - 1)),
                      wait=mw if k == 0 else [])
                    lastm = E("tensor", lambda t, wsl=wsl, k=k, b2=b2: t.matmul(
                        b2[:, :], wsl[:, k, :], xn3[:, k, 515:1027],
                        start=(k == 0), stop=(k == NKT - 1)),
                        inc=T if k == NKT - 1 else None)
                ringB_free(bslot, T, lastm)
                if mt < napairs:
                    adst = big1[:, mt * TOK:(mt + 1) * TOK]
                    E("scalar", lambda s_, adst=adst, b1=b1: s_.activation(
                        adst[:, 0:512], b1[:, :], AF.Silu), wait=[(T, lastm)])
                    nev = E("scalar", lambda s_, adst=adst, b2=b2: s_.activation(
                        adst[:, 512:1024], b2[:, :], AF.Silu), inc=A)
                    hist.append((A, nev))
                else:
                    adst = big1[:, (mt - napairs) * TOK:(mt - napairs + 1) * TOK]
                    E("vector", lambda v, adst=adst, b1=b1: v.tensor_tensor(
                        adst[:, 0:512], adst[:, 0:512], b1[:, :], ALU.mult),
                      wait=[(T, lastm), hist[mt - napairs]])
                    nev = E("vector", lambda v, adst=adst, b2=b2: v.tensor_tensor(
                        adst[:, 512:1024], adst[:, 512:1024], b2[:, :], ALU.mult),
                        inc=V)
                    hist.append((V, nev))
            return hist[-1]

        def do_expert(e):
            UPD = up_tiles(dup, 22, 11, e=e)
            ncw = E("scalar", lambda q, e=e: q.dma_start(cwrow, dcw[e:e + 1, :]),
                    wait=[(SCW, st8["cw_SC"])] +
                    ([(T, st8["cwb_T"])] if st8.get("cwb_T") else []), inc=SCR)
            ncwb_v = None
            for hh in range(2):
                cww = [(SCR, ncw)]
                if hh == 0:
                    cww.append((V, st8["dn_ev_V"]))
                else:
                    cww.append((V, ncwb_v))
                nbb = E("tensor", lambda t, hh=hh: t.matmul(
                    pb[4][:, 0:512], onesall[32:33, 0:128],
                    cwrow[0:1, 512 * hh:512 * (hh + 1)], start=True, stop=True),
                    wait=cww, inc=T)
                st8["cwb_T"] = nbb
                ncwb_v = E("vector", lambda v, hh=hh: v.tensor_copy(
                    cwb_sb[:, 512 * hh:512 * (hh + 1)], pb[4][:, 0:512]),
                    wait=[(T, nbb)], inc=V)
            NCWB = ncwb_v
            for m in range(11):
                E("vector", lambda v, m=m: v.tensor_tensor(
                    big1[:, m * TOK:(m + 1) * TOK], big1[:, m * TOK:(m + 1) * TOK],
                    cwb_sb[:, :], ALU.mult),
                  wait=[(V, NCWB), UPD] if m == 0 else [],
                  inc=V if m == 10 else None)
            NAS = V.n
            for kt in range(11):
                slot, ldn = ring_load(ddn[e, kt])
                hw = [(LDS[slot], ldn)]
                if st8.get(f"dnh_T{kt}"):
                    hw.append((T, st8[f"dnh_T{kt}"]))
                if kt >= 9:
                    hw.append((V, NAS))
                if kt < 9:
                    nh = E("gpsimd", lambda g_, kt=kt, slot=slot: g_.tensor_copy(
                        dn_home(kt)[:, :], ringF[slot][:, 0:2048]), wait=hw, inc=PL)
                    ring_free(slot, PL, nh)
                else:
                    nh = E("vector", lambda v, kt=kt, slot=slot: v.tensor_copy(
                        dn_home(kt)[:, :], ringF[slot][:, 0:2048]), wait=hw, inc=V)
                    ring_free(slot, V, nh)
            NDH = PL.n
            NDHV = prog.ctrs["V"].n
            for tq in range(4):
                mw = [(PL, NDH), (V, NDHV), (V, NAS), (V, st8["dn_ev_V"])]
                lastm = None
                for kt in range(11):
                    for o in range(16):
                        bank, off = dbank(o)
                        lastm = E("tensor", lambda t, kt=kt, o=o, tq=tq, bank=bank,
                                  off=off: t.matmul(
                                      bank[:, off:off + 256],
                                      dn_home(kt)[:, 128 * o:128 * (o + 1)],
                                      big1[:, kt * TOK + 256 * tq:
                                           kt * TOK + 256 * (tq + 1)],
                                      start=(kt == 0 and o % 2 == 0),
                                      stop=(kt == 10), skip_group_check=True),
                                  wait=mw if (kt == 0 and o == 0) else [],
                                  inc=T if (kt == 10 and o == 15) else None)
                nmm = T.n
                for o in range(16):
                    bank, off = dbank(o)
                    E("vector", lambda v, o=o, tq=tq, bank=bank, off=off:
                      v.scalar_tensor_tensor(acc_view(o, tq), bank[:, off:off + 256],
                                             1.0, acc_view(o, tq), ALU.mult,
                                             ALU.add),
                      wait=[(T, nmm)] if o == 0 else [], inc=V if o == 15 else None)
                st8["dn_ev_V"] = V.n
            for kt in range(11):
                st8[f"dnh_T{kt}"] = prog.ctrs["T"].n

        def do_shared():
            UPD = up_tiles(dsup, 44, 22)
            if "a" in ddump:
                E("scalar", lambda q: q.dma_start(ddump["a"], big1[:, 0:22 * TOK]),
                  wait=[UPD], inc=SC)
            for tq in range(4):
                lastm = None
                for kt in range(22):
                    slot, ldn = ring_load(dsdn[kt])
                    bslot, cvn = cast_slab(slot, ldn)
                    wsl = ringB[bslot][:, 0:2048]
                    mw = [(PL, cvn), UPD, (V, st8["dn_ev_V"])]
                    for o in range(16):
                        bank, off = dbank(o)
                        lastm = E("tensor", lambda t, kt=kt, o=o, tq=tq, wsl=wsl,
                                  bank=bank, off=off: t.matmul(
                                      bank[:, off:off + 256],
                                      wsl[:, 128 * o:128 * (o + 1)],
                                      big1[:, kt * TOK + 256 * tq:
                                           kt * TOK + 256 * (tq + 1)],
                                      start=(kt == 0 and o % 2 == 0),
                                      stop=(kt == 21), skip_group_check=True),
                                  wait=mw if o == 0 else [],
                                  inc=T if o == 15 else None)
                    ringB_free(bslot, T, lastm)
                nmm = T.n
                for o in range(16):
                    bank, off = dbank(o)
                    E("vector", lambda v, o=o, tq=tq, bank=bank, off=off:
                      v.scalar_tensor_tensor(acc_view(o, tq), bank[:, off:off + 256],
                                             1.0, acc_view(o, tq), ALU.mult,
                                             ALU.add),
                      wait=[(T, nmm)] if o == 0 else [], inc=V if o == 15 else None)
                st8["dn_ev_V"] = V.n

        def do_finish():
            ACC = st8.get("dn_ev_V", prog.ctrs["V"].n)
            ACCW = [(V, ACC), (A, prog.ctrs["Aa"].n), (PL, prog.ctrs["PL"].n),
                    (V, prog.ctrs["V"].n)]
            for tb in range(8):
                slot, ldn = ring_load(dx[3 + 3 * TOK + 128 * tb:
                                        3 + 3 * TOK + 128 * (tb + 1), :])
                trw = list(ACCW)
                if st8.get("fin_store"):
                    trw.append((ST, st8["fin_store"]))
                for o in range(16):
                    E("tensor", lambda t, o=o, tb=tb: t.transpose(
                        pb[o // 4][:, 128 * (o % 4):128 * (o % 4) + 128],
                        big2[:, o * TOK + 128 * tb:o * TOK + 128 * (tb + 1)],
                        identf[:, :]),
                      wait=trw if o == 0 else [], inc=T if o == 15 else None)
                ntp = T.n
                for q in range(4):
                    E("vector", lambda v, q=q, slot=slot: v.scalar_tensor_tensor(
                        store_sb[:, 512 * q:512 * (q + 1)], pb[q][:, 0:512], 1.0,
                        ringF[slot][:, 512 * q:512 * (q + 1)], ALU.mult, ALU.add),
                      wait=([(T, ntp), (LDS[slot], ldn)] +
                            ([(ST, st8["fin_store"])] if st8.get("fin_store")
                             else [])) if q == 0 else [],
                      inc=V if q == 3 else None)
                nst = V.n
                ring_free(slot, V, nst)
                st8["fin_store"] = E("sync", lambda s_, tb=tb: s_.dma_start(
                    dout[128 * tb:128 * (tb + 1), :], store_sb[:, :]),
                    wait=[(V, nst)], inc=ST)
            E("sync", None, wait=[(ST, prog.ctrs["ST"].n), (SC, prog.ctrs["SC"].n)])

        stop = cfg.get("stop", "all")
        if stop == "consts":
            st8["dn_ev_V"] = prog.ctrs["V"].n
            st8["xn_A_own"] = prog.ctrs["Aa"].n
        else:
            for s in range(NSL - n_slices, NSL):
                do_slice(s)
                if stop == "ln1" and False:
                    break
            if stop not in ("slices", "ln", "ln2", "inproj", "chunk1", "chunk2",
                            "ssm_tr", "ssm_xtm", "ssm_gt", "ssm_z", "ssm_brow",
                            "ssm_yd"):
                do_outproj()
                if stop not in ("outproj",):
                    do_gating()
                    if stop not in ("gating",):
                        for e in range(n_experts):
                            do_expert(e)
                        if do_shared_f:
                            do_shared()
        st8.setdefault("dn_ev_V", prog.ctrs["V"].n)
        if "acc" in ddump:
            E("scalar", lambda q: q.dma_start(ddump["acc"], big2[:, :]),
              wait=[(V, st8["dn_ev_V"])], inc=SC)
        do_finish()

        sems = {}
        for name in prog.ctrs:
            sems[name] = ES.enter_context(nc.semaphore(name))
        block = ES.enter_context(nc.Block())

        def mk(engname):
            def body(engine):
                for w, fn, incname, incstep in prog.streams[engname]:
                    for sname, v in w:
                        engine.wait_ge(sems[sname], v)
                    if fn is None:
                        continue
                    inst = fn(engine)
                    if incname is not None:
                        inst.then_inc(sems[incname], incstep)
            return body

        block.tensor(mk("tensor"))
        block.vector(mk("vector"))
        block.scalar(mk("scalar"))
        block.gpsimd(mk("gpsimd"))
        block.sync(mk("sync"))

    n_inst = sum(len(v) for v in prog.streams.values())
    return nc, n_inst


def _prep_shared(inputs):
    w_in = np.ascontiguousarray(inputs["L0_attn_W_in"], dtype=np.float32)
    w_in_r = w_in.reshape(NKT, 128, OUT_IN)
    w_in_p = np.zeros((67, 128, NKT * 128), dtype=np.float32)
    for mtile in range(67):
        c0, c1 = 128 * mtile, min(128 * mtile + 128, OUT_IN)
        blk = w_in_r[:, :, c0:c1].transpose(1, 0, 2)
        tmp = np.zeros((128, NKT, 128), dtype=np.float32)
        tmp[:, :, 0:c1 - c0] = blk
        w_in_p[mtile] = tmp.reshape(128, NKT * 128)
    conv_w = inputs["L0_attn_conv_w"].astype(np.float32)
    conv_w_fm = conv_w.T.reshape(34, 128, 4).transpose(1, 0, 2).reshape(128, 136)
    conv_b_fm = inputs["L0_attn_conv_b"].astype(np.float32).reshape(34, 128).T
    D = inputs["L0_attn_D"].astype(np.float32)
    f = np.arange(4096)
    d_exp = np.zeros((128, 32), dtype=np.float32)
    d_exp[f % 128, f // 128] = D[f // 64]
    gamma_fm = inputs["L0_attn_norm_weight"].astype(np.float32).reshape(32, 128).T
    w_out_dev = np.ascontiguousarray(
        inputs["L0_attn_W_out"], dtype=np.float32).reshape(32, 128, HID)
    gate_dev = inputs["L0_ffn_gate_w"].astype(np.float32).T.reshape(
        NKT, 128, 16).transpose(1, 0, 2).reshape(128, NKT * 16)
    up = inputs["L0_ffn_expert_up_w"].astype(np.float32)
    up_dev = np.zeros((NE, 22, 128, NKT * 128), dtype=np.float32)
    for e in range(NE):
        upT = np.ascontiguousarray(up[e].T).reshape(NKT, 128, 2816)
        up_dev[e] = upT.reshape(NKT, 128, 22, 128).transpose(2, 1, 0, 3).reshape(
            22, 128, NKT * 128)
    down_dev = np.ascontiguousarray(
        inputs["L0_ffn_expert_down_w"].astype(np.float32).transpose(0, 2, 1)
    ).reshape(NE, 11, 128, HID)
    supT = np.ascontiguousarray(
        inputs["L0_ffn_shared_up_w"].astype(np.float32).T).reshape(NKT, 128, 5632)
    shup_dev = np.ascontiguousarray(
        supT.reshape(NKT, 128, 44, 128).transpose(2, 1, 0, 3)).reshape(
        44, 128, NKT * 128)
    shdn_dev = np.ascontiguousarray(
        inputs["L0_ffn_shared_down_w"].astype(np.float32).T).reshape(22, 128, HID)
    tri = (np.arange(128)[None, :] >= np.arange(128)[:, None]).astype(np.float32)
    consts = np.zeros((128, 8), dtype=np.float32)
    consts[:, 0] = 1.0
    consts[:, 1] = EPS
    consts[:, 3] = 1.0 / 4096.0
    return dict(
        w_in_dev=w_in_p, conv_w_fm=np.ascontiguousarray(conv_w_fm),
        conv_b_fm=np.ascontiguousarray(conv_b_fm),
        dt_bias=inputs["L0_attn_dt_bias"].astype(np.float32).reshape(64, 1),
        a_log=inputs["L0_attn_A_log"].astype(np.float32).reshape(64, 1),
        d_exp=d_exp, gamma_fm=np.ascontiguousarray(gamma_fm),
        w_out_dev=w_out_dev, gate_dev=np.ascontiguousarray(gate_dev),
        up_dev=up_dev, down_dev=down_dev, shup_dev=shup_dev, shdn_dev=shdn_dev,
        tri=tri, identb=np.eye(128).astype(ml_dtypes.bfloat16),
        identf=np.eye(128, dtype=np.float32), consts=consts,
        ones_row=np.ones((1, 128), dtype=np.float32),
        ones_all=np.ones((128, 128), dtype=np.float32),
        ones_col_b=np.ones((128, 1), dtype=ml_dtypes.bfloat16),
    )


def _prep_core_inputs(core, inputs, shared_prep):
    x = inputs["L0_attn_hidden_states"]
    s, m = core // 4, core % 4
    start = 1024 * m
    lo = start - 3072 - 3
    xc = np.zeros((XROWS, HID), dtype=np.float32)
    src_lo = max(lo, 0)
    xc[src_lo - lo:] = np.asarray(x[s, src_lo:start + 1024], dtype=np.float32)
    mask = np.zeros((XROWS,), dtype=np.float32)
    mask[max(0, -lo):] = 1.0
    mask_ln = np.zeros((128, NSL * 9), dtype=np.float32)
    for sl in range(NSL):
        for tt in range(9):
            rows = 128 if tt < 8 else 3
            r0 = 1024 * sl + 128 * tt
            mask_ln[0:rows, sl * 9 + tt] = mask[r0:r0 + rows]
    mask_blk = np.zeros((128, NSL * 8), dtype=np.float32)
    for sl in range(NSL):
        for b in range(8):
            r0 = 1024 * sl + 3 + 128 * b
            mask_blk[:, sl * 8 + b] = mask[r0:r0 + 128]
    d = dict(shared_prep)
    d["x_core"] = xc
    d["mask_ln"] = mask_ln
    d["mask_blk"] = mask_blk
    return d


def kernel(**inputs):
    shared_prep = _prep_shared(inputs)
    in_maps = [_prep_core_inputs(k, inputs, shared_prep) for k in range(8)]
    nc, n_inst = build_nc()
    res = run_bass_kernel_spmd(nc, in_maps, core_ids=list(range(8)))
    out = np.zeros((2, 4096, HID), dtype=np.float32)
    for k in range(8):
        s, m = k // 4, k % 4
        out[s, 1024 * m:1024 * (m + 1)] = res.results[k]["out"]
    return out
